# version 01 (final)
# baseline (speedup 1.0000x reference)
import numpy as np
import ml_dtypes
import concourse.bass as bass
import concourse.mybir as mybir
from contextlib import ExitStack

F32, BF16 = mybir.dt.float32, mybir.dt.bfloat16
AF, ALU, AX = mybir.ActivationFunctionType, mybir.AluOpType, mybir.AxisListType.X
MAGIC = float(np.float32(1.5 * 2.0**23))
BF = ml_dtypes.bfloat16

D, S, T = 1024, 4096, 2048
HA = 16
DHID, DE = 4096, 2048
EPS = float(np.float32(1e-5))
NTF, NTO, NBF, NBO = 32, 16, 8, 4
ENGINES = ("sync", "gpsimd", "vector", "scalar", "tensor")


class Chan:
    def __init__(self, name):
        self.name, self.sem, self.n = name, None, 0


def build(stage=4, dbg=()):
    nc = bass.Bass("TRN2", target_bir_lowering=False, debug=False)
    es = ExitStack()
    streams = {e: [] for e in ENGINES}

    CH = {nm: Chan(nm) for nm in (
        "Ds", "Dg", "Dd", "Tm", "TAV", "PEXP",
        "PAB", "PSx", "POx", "PTx", "PRx", "Vm", "Sm", "AG", "VS", "SS", "Dd0", "Dd1",
        "Dlnb", "Dxb", "Dx", "Dkvq", "Dw", "Dwg", "Dh0", "Dh1", "Dhr", "Dn", "Dbt", "Dbt1")}

    def emit(eng, fn):
        streams[eng].append(fn)

    import os as _os
    PLANLOG = _os.environ.get("PLANLOG") == "1"
    plog = open("/tmp/plan.log", "w") if PLANLOG else None

    def w(eng, ch, val=None):
        v = ch.n if val is None else val
        if plog: plog.write(f"WAIT {eng} {ch.name} {v}\n")
        if v > 0:
            emit(eng, lambda E, c=ch, vv=v: E.wait_ge(c.sem, vv))

    import os
    DMA_DEBUG = os.environ.get("DMA_DEBUG") == "1"

    def _icount():
        return sum(len(b.instructions) for b in nc.m.functions[0].blocks)

    def dma(eng, out, in_, ch=None):
        c = ch if ch is not None else (CH["Ds"] if eng == "sync" else CH["Dg"])
        if DMA_DEBUG:
            def f(E, o=out, i=in_, cc=c):
                a = _icount()
                E.dma_start(out=o, in_=i).then_inc(cc.sem, 16)
                d = _icount() - a
                if d != 1:
                    print(f"DMA SPLIT x{d}: out={o.tensor.name} in={i.tensor.name}", flush=True)
            emit(eng, f)
        else:
            emit(eng, lambda E, o=out, i=in_, cc=c: E.dma_start(out=o, in_=i).then_inc(cc.sem, 16))
        c.n += 16
        if plog: plog.write(f"DMA {eng} {c.name} ->{c.n} out={getattr(out.tensor,'name','?')}\n")

    def op(eng, meth, ch=None, **kw):
        if ch is None:
            emit(eng, lambda E, m=meth, kw=kw: getattr(E, m)(**kw))
        else:
            emit(eng, lambda E, m=meth, kw=kw, c=ch: getattr(E, m)(**kw).then_inc(c.sem, 1))
            ch.n += 1

    def mm(out, lhsT, rhs, start, stop, ch=None):
        if ch is None:
            emit("tensor", lambda E, o=out, l=lhsT, r=rhs, s1=start, s2=stop:
                 E.matmul(o, l, r, start=s1, stop=s2))
        else:
            emit("tensor", lambda E, o=out, l=lhsT, r=rhs, s1=start, s2=stop, c=ch:
                 E.matmul(o, l, r, start=s1, stop=s2).then_inc(c.sem, 1))
            ch.n += 1

    CHS = {"vector": CH["VS"], "scalar": CH["SS"]}

    def ser(eng, meth, **kw):
        c = CHS[eng]
        w(eng, c)
        op(eng, meth, ch=c, **kw)

    def din(name, shape, dt=F32):
        return nc.dram_tensor(name, list(shape), dt, kind="ExternalInput").ap()

    I = {}
    I["x_own"] = din("x_own", [T, D])
    I["x_full"] = din("x_full", [S, D])
    for wn in ("q", "k", "v", "o"):
        I[f"aW{wn}T"] = din(f"aW{wn}T", [D, D])
        I[f"aA{wn}T"] = din(f"aA{wn}T", [D, 8])
        I[f"aB{wn}T"] = din(f"aB{wn}T", [8, D])
    I["abias"] = din("abias", [1, 4 * D])
    I["rz_col"] = din("rz_col", [128, 1])
    I["w1T"] = din("w1T", [D, DHID])
    I["w2T"] = din("w2T", [DHID, D])
    I["convw"] = din("convw", [128, 3 * 32])
    I["convb"] = din("convb", [128, 32])
    I["bWq_my"] = din("bWq_my", [D, 1024])
    I["bWk_my"] = din("bWk_my", [D, 1024])
    I["bWin_v"] = din("bWin_v", [D, 1024])
    I["bWin_p"] = din("bWin_p", [D, DE])
    I["bWout"] = din("bWout", [DE, D])
    I["b_lnw"] = din("b_lnw", [128, 24])
    I["b_lnb"] = din("b_lnb", [128, 24])
    I["b_bqk_row"] = din("b_bqk_row", [1, 2048], BF16)
    I["qpre"] = din("qpre", [8, 512])
    I["b_outw_row"] = din("b_outw_row", [1, D])
    I["b_outb_row"] = din("b_outb_row", [1, D])
    I["b_smear_row"] = din("b_smear_row", [1, 8])
    I["b_lsc_row"] = din("b_lsc_row", [1, 8])
    I["identf"] = din("identf", [128, 128])
    I["atable"] = din("atable", [128, HA * 32])
    I["btable"] = din("btable", [128, 8 * 144])
    I["btile"] = din("btile", [147 * 128, 512], BF16)
    I["trilm"] = din("trilm", [128, 5 * 512])
    I["selhalo"] = din("selhalo", [128, 4])
    I["selAB"] = din("selAB", [128, 2])

    OUT = {"y": nc.dram_tensor("y", [T, D], F32, kind="ExternalOutput").ap()}
    DBG_SHAPES = {"dbg_x1": ([T, D], F32), "dbg_K": ([D, S], BF16),
                  "dbg_V": ([S, D], BF16), "dbg_Q": ([D, T], BF16),
                  "dbg_h": ([32, 128, T], F32), "dbg_h2": ([32, 128, T], F32),
                  "dbg_w1q": ([D, DHID], BF16), "dbg_rm": ([128, 512], F32),
                  "dbg_hin": ([128, 1028], F32), "dbg_h2s": ([128, 512], F32)}
    for nm in dbg:
        shp, dt = DBG_SHAPES[nm]
        OUT[nm] = nc.dram_tensor(nm, shp, dt, kind="ExternalOutput").ap()

    SC = {}
    SC["ln0f"] = nc.dram_tensor("sc_ln0f", [D, S], BF16).ap()
    SC["ln0o"] = nc.dram_tensor("sc_ln0o", [D, T], BF16).ap()
    SC["Ka"] = nc.dram_tensor("sc_Ka", [D, S], BF16).ap()
    SC["Va"] = nc.dram_tensor("sc_Va", [S, D], BF16).ap()
    SC["Qa"] = nc.dram_tensor("sc_Qa", [D, T], BF16).ap()
    SC["oa"] = nc.dram_tensor("sc_oa", [D, T], BF16).ap()
    SC["h"] = nc.dram_tensor("sc_h", [32, 128, T], F32).ap()
    SC["h2"] = nc.dram_tensor("sc_h2", [32, 128, T], F32).ap()
    SC["w1q"] = nc.dram_tensor("sc_w1q", [D, DHID], BF16).ap()
    SC["w2q"] = nc.dram_tensor("sc_w2q", [DHID, D], BF16).ap()
    SC["x2lnf"] = nc.dram_tensor("sc_x2lnf", [D, S], BF16).ap()
    SC["x2lno"] = nc.dram_tensor("sc_x2lno", [D, T], BF16).ap()
    SC["qps"] = nc.dram_tensor("sc_qps", [8, 512], F32).ap()
    SC["qb"] = nc.dram_tensor("sc_qb", [8 * 128, S], BF16).ap()
    SC["kb"] = nc.dram_tensor("sc_kb", [8 * 128, S], BF16).ap()
    SC["vb"] = nc.dram_tensor("sc_vb", [S, 8 * 128], BF16).ap()
    SC["pb"] = nc.dram_tensor("sc_pb", [DE, T], BF16).ap()
    SC["ag1i"] = nc.dram_tensor("ag1i", [128, 64], F32).ap()
    SC["ag1o"] = nc.dram_tensor("ag1o", [2, 128, 64], F32).ap()
    SC["ag2i"] = nc.dram_tensor("ag2i", [T, D], F32).ap()
    SC["ag2o"] = nc.dram_tensor("ag2o", [S, D], F32).ap()
    SC["ag3i"] = nc.dram_tensor("ag3i", [8 * 128, S], BF16).ap()
    SC["ag3o"] = nc.dram_tensor("ag3o", [16 * 128, S], BF16).ap()
    SC["ag4i"] = nc.dram_tensor("ag4i", [128, 64], F32).ap()
    SC["ag4o"] = nc.dram_tensor("ag4o", [2, 128, 64], F32).ap()

    SB = {}
    def sb(name, shape, dt=F32):
        SB[name] = es.enter_context(nc.sbuf_tensor(name, list(shape), dt))
        return SB[name]

    identf = sb("sb_identf", [128, 128])
    identb = sb("sb_identb", [128, 128], BF16)
    onesb = sb("onesb", [1, 512], BF16)
    onesf = sb("onesf", [1, 512])
    rz = sb("rz", [128, 1])
    stats = sb("stats", [128, 32])
    lnstat = sb("lnstat", [128, 2 * NTF])
    lnstatow = sb("lnstatow", [128, 2 * NTO])
    atab = sb("sb_atab", [128, HA * 32])
    btab = sb("sb_btab", [128, 8 * 144])
    convw = sb("sb_convw", [128, 3 * 32])
    convb = sb("sb_convb", [128, 32])
    blw = sb("sb_blw", [128, 24])
    blb = sb("sb_blb", [128, 24])
    bqkb = sb("sb_bqkb", [1, 2048], BF16)
    onescb = sb("onescb", [128, 1], BF16)
    neg16 = sb("neg16", [128, 1])
    smearc = sb("smearc", [128, 8 * 4])
    selhalo = sb("sb_selhalo", [128, 4])
    wsc = sb("wsc", [128, 4])
    spc = sb("spc", [128, 4])
    rowb = sb("rowb", [1, 1792])
    rowb2 = sb("rowb2", [1, 1792])
    u4 = sb("u4", [8, 4 * 512], BF16)
    ptile = sb("ptile", [128, 2 * 512], BF16)
    Vhead = sb("Vhead", [128, 32 * 65], BF16)
    Osb = sb("Osb", [128, 512])
    rrep = sb("rrep", [128, 512])
    evb = sb("evb", [128, 2 * 512], BF16)
    ev = sb("ev", [128, D])
    tt = sb("tt", [128, D])
    tt2 = sb("tt2", [128, D])
    junk = sb("junk", [128, 2 * D])
    ones64 = sb("ones64", [128, 1])
    SB["kcarry"] = sb("kcarry", [128, 8])
    SB["ktmp"] = sb("ktmp", [128, 512])
    SB["ktmp2"] = sb("ktmp2", [128, 512])
    SB["selAB"] = sb("sb_selAB", [128, 2])
    SB["lnstat2"] = sb("lnstat2", [128, 4])
    xres = sb("xres", [128, NTO * D])
    ARN = sb("arena", [128, 30 * 1024], BF16)

    def av(off_kb, cols, dt=BF16):
        c0 = int(off_kb * 1024) // 2
        if dt == BF16:
            return ARN[:, c0:c0 + cols]
        return ARN[:, c0:c0 + 2 * cols].bitcast(F32)

    PS = {}
    for nm in ("psA", "psB", "psO", "psS", "psR", "psT", "psU", "psX"):
        PS[nm] = es.enter_context(nc.psum_tensor(nm, [128, 512], F32))

    def recip(out_ap, in_ap, ch=None):
        op("vector", "tensor_copy", out=spc[:, 0:1], in_=spc[:, 1:2])
        op("vector", "reciprocal", ch=ch, out=out_ap, in_=in_ap)

    def ln_block(xsrc_fn, statbuf, stile0, nt, xb, lb):
        w("sync", CH["Vm"])
        for i in range(nt):
            dma("sync", xb[:, i * D:(i + 1) * D], xsrc_fn(i), CH["Dxb"])
        w("scalar", CH["Vm"])
        w("scalar", CH["Dxb"])
        w("vector", CH["Dxb"])
        for i in range(nt):
            xt = xb[:, i * D:(i + 1) * D]
            ser("scalar", "activation", out=junk[:, i * D:(i + 1) * D], in_=xt,
                func=AF.Square, accum_out=stats[:, 4 + i:5 + i])
            ser("vector", "reduce_sum", out=stats[:, 0 + i:1 + i], in_=xt, axis=AX)
        w("vector", CH["SS"])
        for i in range(nt):
            c0 = 2 * (stile0 + i)
            ser("vector", "tensor_scalar", out=statbuf[:, c0:c0 + 1],
                in0=stats[:, i:i + 1], scalar1=-1.0 / D, scalar2=None, op0=ALU.mult)
            ser("vector", "tensor_tensor", out=stats[:, 8 + i:9 + i],
                in0=statbuf[:, c0:c0 + 1], in1=statbuf[:, c0:c0 + 1], op=ALU.mult)
            ser("vector", "scalar_tensor_tensor", out=stats[:, 12 + i:13 + i],
                in0=stats[:, 4 + i:5 + i], scalar=1.0 / D, in1=stats[:, 8 + i:9 + i],
                op0=ALU.mult, op1=ALU.subtract)
            ser("vector", "tensor_scalar", out=stats[:, 14 + i:15 + i],
                in0=stats[:, 12 + i:13 + i], scalar1=EPS, scalar2=None, op0=ALU.add)
        ser("vector", "reciprocal", out=stats[:, 16:16 + nt], in_=stats[:, 14:14 + nt])
        ser("vector", "tensor_copy", out=stats[:, 20:20 + nt], in_=stats[:, 16:16 + nt])
        w("vector", CH["VS"])
        op("vector", "engine_nop", ch=CH["Vm"])
        w("scalar", CH["Vm"])
        ser("scalar", "activation", out=stats[:, 24:24 + nt],
            in_=stats[:, 20:20 + nt], func=AF.Sqrt)
        w("vector", CH["SS"])
        for i in range(nt):
            c0 = 2 * (stile0 + i)
            ser("vector", "tensor_copy", out=statbuf[:, c0 + 1:c0 + 2], in_=stats[:, 24 + i:25 + i])
        for i in range(nt):
            c0 = 2 * (stile0 + i)
            ser("vector", "tensor_scalar", out=lb[:, i * D:(i + 1) * D],
                in0=xb[:, i * D:(i + 1) * D],
                scalar1=statbuf[:, c0:c0 + 1], scalar2=statbuf[:, c0 + 1:c0 + 2],
                op0=ALU.add, op1=ALU.mult)
        w("vector", CH["VS"])
        op("vector", "engine_nop", ch=CH["Vm"])

    def ln_block_sb(src_fn, statbuf, stile0, nt, lb):
        w("vector", CH["Vm"])
        w("scalar", CH["Vm"])
        for i in range(nt):
            xt = src_fn(i)
            ser("scalar", "activation", out=junk[:, i * D:(i + 1) * D], in_=xt,
                func=AF.Square, accum_out=stats[:, 4 + i:5 + i])
            ser("vector", "reduce_sum", out=stats[:, 0 + i:1 + i], in_=xt, axis=AX)
        w("vector", CH["SS"])
        for i in range(nt):
            c0 = 2 * (stile0 + i)
            ser("vector", "tensor_scalar", out=statbuf[:, c0:c0 + 1],
                in0=stats[:, i:i + 1], scalar1=-1.0 / D, scalar2=None, op0=ALU.mult)
            ser("vector", "tensor_tensor", out=stats[:, 8 + i:9 + i],
                in0=statbuf[:, c0:c0 + 1], in1=statbuf[:, c0:c0 + 1], op=ALU.mult)
            ser("vector", "scalar_tensor_tensor", out=stats[:, 12 + i:13 + i],
                in0=stats[:, 4 + i:5 + i], scalar=1.0 / D, in1=stats[:, 8 + i:9 + i],
                op0=ALU.mult, op1=ALU.subtract)
            ser("vector", "tensor_scalar", out=stats[:, 14 + i:15 + i],
                in0=stats[:, 12 + i:13 + i], scalar1=EPS, scalar2=None, op0=ALU.add)
        ser("vector", "reciprocal", out=stats[:, 16:16 + nt], in_=stats[:, 14:14 + nt])
        ser("vector", "tensor_copy", out=stats[:, 20:20 + nt], in_=stats[:, 16:16 + nt])
        w("vector", CH["VS"])
        op("vector", "engine_nop", ch=CH["Vm"])
        w("scalar", CH["Vm"])
        ser("scalar", "activation", out=stats[:, 24:24 + nt],
            in_=stats[:, 20:20 + nt], func=AF.Sqrt)
        w("vector", CH["SS"])
        for i in range(nt):
            c0 = 2 * (stile0 + i)
            ser("vector", "tensor_copy", out=statbuf[:, c0 + 1:c0 + 2], in_=stats[:, 24 + i:25 + i])
        for i in range(nt):
            c0 = 2 * (stile0 + i)
            ser("vector", "tensor_scalar", out=lb[:, i * D:(i + 1) * D],
                in0=src_fn(i),
                scalar1=statbuf[:, c0:c0 + 1], scalar2=statbuf[:, c0 + 1:c0 + 2],
                op0=ALU.add, op1=ALU.mult)
        w("vector", CH["VS"])
        op("vector", "engine_nop", ch=CH["Vm"])

    def transpose_to_sbuf(lb, ntile0, nt, dest, destw, dcol0):
        w("tensor", CH["Vm"])
        for j in range(8 * nt):
            ic, i = j // nt, j % nt
            bank = PS["psT"] if j % 2 == 0 else PS["psU"]
            w("tensor", CH["PTx"], CH["PTx"].n - 1)
            mm(bank[:, 0:128], lb[:, (ntile0 + i) * D + ic * 128: (ntile0 + i) * D + (ic + 1) * 128],
               identb[:] if lb.dtype == BF16 else identf[:], True, True, CH["Tm"])
            w("vector", CH["Tm"])
            op("vector", "tensor_copy", ch=CH["PTx"],
               out=dest[:, ic * destw + dcol0 + i * 128: ic * destw + dcol0 + (i + 1) * 128],
               in_=bank[:, 0:128])

    def transpose_block_to_scratch(lb, nt, dst_dram, col0):
        TW = nt * 128
        w("tensor", CH["Vm"])
        w("vector", CH["Dd"])
        for j in range(8 * nt):
            ic, i = j // nt, j % nt
            bank = PS["psT"] if j % 2 == 0 else PS["psU"]
            w("tensor", CH["PTx"], CH["PTx"].n - 1)
            mm(bank[:, 0:128], lb[:, i * D + ic * 128: i * D + (ic + 1) * 128], identf[:],
               True, True, CH["Tm"])
            w("vector", CH["Tm"])
            op("vector", "tensor_copy", ch=CH["PTx"],
               out=lnb0[:, ic * TW + i * 128: ic * TW + (i + 1) * 128], in_=bank[:, 0:128])
        w("gpsimd", CH["PTx"])
        dma("gpsimd",
            dst_dram.rearrange("(c p) t -> p c t", p=128)[:, :, col0:col0 + TW],
            lnb0[:, 0:8 * TW].rearrange("p (c f) -> p c f", f=TW), CH["Dd"])

    op("vector", "memset", ap=onesb[:], constant=1.0)
    op("vector", "memset", ap=onesf[:], constant=1.0)
    op("vector", "memset", ap=ones64[:], constant=1.0)
    op("vector", "memset", ap=onescb[:], constant=1.0)
    op("vector", "memset", ap=neg16[:], constant=-16.0)
    op("vector", "memset", ap=stats[:], constant=0.0)
    for kt in range(32):
        op("vector", "memset", ap=Vhead[:, kt * 65 + 64: kt * 65 + 65], constant=1.0)
    dma("sync", identf[:], I["identf"])
    dma("gpsimd", identb[:], I["identf"])
    dma("sync", rz[:], I["rz_col"])
    dma("sync", atab[:], I["atable"])
    dma("sync", btab[:], I["btable"])
    dma("sync", convw[:], I["convw"])
    dma("sync", convb[:], I["convb"])
    dma("sync", blw[:], I["b_lnw"])
    dma("sync", blb[:], I["b_lnb"])
    dma("sync", selhalo[:], I["selhalo"])
    dma("sync", SB["selAB"][:], I["selAB"])
    dma("gpsimd", bqkb[:], I["b_bqk_row"])

    aWk = av(0, 8 * D)
    aA = av(16, 4 * 8 * 8, BF16)
    aB = av(17, 4 * D)
    aB2 = av(25, 4 * D)
    lnb0 = av(41, 8 * 512)
    lnb1 = av(49, 8 * 512)
    lnt = sb("lnt", [128, 2 * D])
    xblk = av(0, 2 * D, F32)

    for ki, wn in enumerate(("q", "k", "v", "o")):
        for ic in range(8):
            dma("gpsimd", aA[:, (ki * 8 + ic) * 8:(ki * 8 + ic + 1) * 8],
                I[f"aA{wn}T"][ic * 128:(ic + 1) * 128, :])
        dma("gpsimd", aB[0:8, ki * D:(ki + 1) * D], I[f"aB{wn}T"])
    abias2 = av(33, 4 * D)[0:1, :]
    dma("gpsimd", abias2, I["abias"])
    w("vector", CH["Dg"])
    op("vector", "tensor_scalar", ch=CH["Vm"], out=aB2[0:8, 0:4 * D], in0=aB[0:8, 0:4 * D],
       scalar1=0.125, scalar2=None, op0=ALU.mult)

    PH = int(_os.environ.get("BUILD_PH", "99"))
    for e in ("tensor", "scalar", "vector", "gpsimd", "sync"):
        w(e, CH["Ds"])
        w(e, CH["Dg"])
        w(e, CH["Vm"])

    SUB = int(_os.environ.get("BUILD_SUB", "99"))
    nbf1 = (NBF if SUB >= 3 else 1) if PH >= 1 else 0
    for blk in range(nbf1):
        for g in range(2 if SUB >= 2 else 1):
            t0 = blk * 4 + g * 2
            ln_block(lambda i, t_=t0: I["x_full"][(t_ + i) * 128:(t_ + i + 1) * 128, :],
                     lnstat, t0, 2, xblk, lnt)
            if SUB >= 1:
                transpose_block_to_scratch(lnt, 2, SC["ln0f"], t0 * 128)
    for blk in range(NBO if PH >= 2 else 0):
        for g in range(2):
            t0 = blk * 4 + g * 2
            ln_block(lambda i, t_=t0: I["x_own"][(t_ + i) * 128:(t_ + i + 1) * 128, :],
                     lnstatow, t0, 2, xblk, lnt)
            transpose_block_to_scratch(lnt, 2, SC["ln0o"], t0 * 128)

    def proj_pass(kind, wname, src_dram, src_cols, dst, dst_kind, nblocks):
        for ic in range(8):
            dma("gpsimd", aWk[:, ic * D:(ic + 1) * D], I[f"aW{wname}T"][ic * 128:(ic + 1) * 128, :])
        w("tensor", CH["Dg"])
        w("sync", CH["Dd"])
        for blk in range(nblocks):
            lnb = lnb0 if blk % 2 == 0 else lnb1
            w("sync", CH["Tm"])
            dma("sync", lnb.rearrange("p (c f) -> p c f", f=512),
                src_dram.rearrange("(c p) t -> p c t", p=128)[:, :, blk * 512:(blk + 1) * 512],
                CH["Dlnb"])
            w("tensor", CH["Dlnb"])
            w("tensor", CH["PSx"])
            for ic in range(8):
                mm(PS["psS"][0:8, 0:512], aA[:, (kind * 8 + ic) * 8:(kind * 8 + ic + 1) * 8],
                   lnb[:, ic * 512:(ic + 1) * 512], ic == 0, ic == 7, CH["Tm"] if ic == 7 else None)
            w("vector", CH["Tm"])
            op("vector", "tensor_copy", ch=CH["PSx"], out=u4[0:8, kind * 512:(kind + 1) * 512],
               in_=PS["psS"][0:8, 0:512])
            w("tensor", CH["PSx"])
            if dst_kind == "fm":
                for oc in range(8):
                    pb = PS["psA"] if oc % 2 == 0 else PS["psB"]
                    w("tensor", CH["PAB"], CH["PAB"].n - 1)
                    for ic in range(8):
                        mm(pb[:, 0:512], aWk[:, ic * D + oc * 128: ic * D + (oc + 1) * 128],
                           lnb[:, ic * 512:(ic + 1) * 512], ic == 0, False)
                    mm(pb[:, 0:512], aB2[0:8, kind * D + oc * 128: kind * D + (oc + 1) * 128],
                       u4[0:8, kind * 512:(kind + 1) * 512], False, False)
                    mm(pb[:, 0:512], abias2[0:1, kind * D + oc * 128: kind * D + (oc + 1) * 128],
                       onesb[0:1, 0:512], False, True, CH["Tm"])
                    sl = oc % 2
                    w("scalar", CH["Tm"])
                    w("scalar", CH["Dd0" if sl == 0 else "Dd1"])
                    op("scalar", "activation", ch=CH["PAB"], out=evb[:, sl * 512:(sl + 1) * 512],
                       in_=pb[:, 0:512], func=AF.Identity)
                    w("gpsimd", CH["PAB"])
                    dma("gpsimd", dst[oc * 128:(oc + 1) * 128, blk * 512:(blk + 1) * 512],
                        evb[:, sl * 512:(sl + 1) * 512], CH["Dd0" if sl == 0 else "Dd1"])
            else:
                for i in range(4):
                    for half in range(2):
                        pb = PS["psA"] if half == 0 else PS["psB"]
                        w("tensor", CH["PAB"], CH["PAB"].n - 1)
                        for ic in range(8):
                            mm(pb[:, 0:512], lnb[:, ic * 512 + i * 128: ic * 512 + (i + 1) * 128],
                               aWk[:, ic * D + half * 512: ic * D + (half + 1) * 512], ic == 0, False)
                        mm(pb[:, 0:512], u4[0:8, kind * 512 + i * 128: kind * 512 + (i + 1) * 128],
                           aB2[0:8, kind * D + half * 512: kind * D + (half + 1) * 512], False, False)
                        mm(pb[:, 0:512], onesb[0:1, 0:128],
                           abias2[0:1, kind * D + half * 512: kind * D + (half + 1) * 512],
                           False, True, CH["Tm"])
                        sl = half
                        w("scalar", CH["Tm"])
                        w("scalar", CH["Dd0" if sl == 0 else "Dd1"])
                        op("scalar", "activation", ch=CH["PAB"], out=evb[:, sl * 512:(sl + 1) * 512],
                           in_=pb[:, 0:512], func=AF.Identity)
                        w("gpsimd", CH["PAB"])
                        dma("gpsimd", dst[(blk * 4 + i) * 128:(blk * 4 + i + 1) * 128,
                                          half * 512:(half + 1) * 512],
                            evb[:, sl * 512:(sl + 1) * 512], CH["Dd0" if sl == 0 else "Dd1"])

    if PH >= 3:
        proj_pass(1, "k", SC["ln0f"], S, SC["Ka"], "fm", NBF)
    if PH >= 4:
        proj_pass(2, "v", SC["ln0f"], S, SC["Va"], "tm", NBF)
    if PH >= 5:
        proj_pass(0, "q", SC["ln0o"], T, SC["Qa"], "fm", NBO)

    Khead = av(0, S)
    Qhead = av(8, T)
    w("sync", CH["Tm"])
    w("sync", CH["Dd"])
    w("sync", CH["Dd0"])
    w("sync", CH["Dd1"])
    for h in range(HA if PH >= 6 else 0):
        hp, hq = h // 2, h % 2
        w("sync", CH["Tm"])
        w("sync", CH["Dd0"])
        w("sync", CH["Dd1"])
        dma("sync", Khead[0:64, 0:S], SC["Ka"][h * 64:(h + 1) * 64, :], CH["Dkvq"])
        dma("sync", Vhead[:, 0:32 * 65].rearrange("p (t c) -> p t c", c=65)[:, :, 0:64],
            SC["Va"][:, h * 64:(h + 1) * 64].rearrange("(t p) c -> p t c", p=128), CH["Dkvq"])
        dma("sync", Qhead[0:64, 0:T], SC["Qa"][h * 64:(h + 1) * 64, :], CH["Dkvq"])
        w("tensor", CH["Dkvq"])
        for qc in range(4):
            w("tensor", CH["POx"])
            for kt in range(32):
                pb = PS["psA"] if kt % 2 == 0 else PS["psB"]
                w("tensor", CH["PEXP"], CH["PEXP"].n - 1)
                mm(pb[:, 0:512], Khead[0:64, kt * 128:(kt + 1) * 128],
                   Qhead[0:64, qc * 512:(qc + 1) * 512], True, True, CH["Tm"])
                w("scalar", CH["Tm"])
                w("scalar", CH["TAV"], CH["TAV"].n - 1)
                op("scalar", "activation", ch=CH["PEXP"],
                   out=ptile[:, (kt % 2) * 512:(kt % 2 + 1) * 512],
                   in_=pb[:, 0:512], func=AF.Exp,
                   bias=atab[:, h * 32 + kt: h * 32 + kt + 1], scale=0.125)
                w("tensor", CH["PEXP"])
                mm(PS["psO"][0:65, 0:512], Vhead[:, kt * 65:(kt + 1) * 65],
                   ptile[:, (kt % 2) * 512:(kt % 2 + 1) * 512], kt == 0, kt == 31, CH["TAV"])
            w("vector", CH["TAV"])
            ser("vector", "tensor_copy", out=Osb[0:65, 0:512], in_=PS["psO"][0:65, 0:512])
            w("vector", CH["VS"])
            op("vector", "engine_nop", ch=CH["POx"])
            w("tensor", CH["POx"])
            w("tensor", CH["PSx"])
            mm(PS["psS"][0:1, 0:512], ones64[64:65, 0:1], Osb[64:65, 0:512], True, True, CH["Tm"])
            w("vector", CH["Tm"])
            ser("vector", "tensor_copy", out=rowb[0:1, 0:512], in_=PS["psS"][0:1, 0:512])
            w("vector", CH["VS"])
            op("vector", "engine_nop", ch=CH["PSx"])
            ser("vector", "reciprocal", out=rowb2[0:1, 0:512], in_=rowb[0:1, 0:512])
            w("vector", CH["VS"])
            op("vector", "engine_nop", ch=CH["Vm"])
            w("tensor", CH["Vm"])
            w("tensor", CH["PRx"])
            mm(PS["psR"][:, 0:512], onesf[0:1, 0:128], rowb2[0:1, 0:512], True, True, CH["Tm"])
            w("vector", CH["Tm"])
            ser("vector", "tensor_copy", out=rrep[:, 0:512], in_=PS["psR"][:, 0:512])
            w("vector", CH["VS"])
            op("vector", "engine_nop", ch=CH["PRx"])
            w("vector", CH["Dd0"])
            ser("vector", "tensor_tensor", out=evb[0:64, 0:512],
                in0=Osb[0:64, 0:512], in1=rrep[0:64, 0:512], op=ALU.mult)
            w("vector", CH["VS"])
            op("vector", "engine_nop", ch=CH["Vm"])
            w("gpsimd", CH["Vm"])
            dma("gpsimd", SC["oa"][h * 64:(h + 1) * 64, qc * 512:(qc + 1) * 512],
                evb[0:64, 0:512], CH["Dd0"])

    w("gpsimd", CH["Tm"])
    for ic in range(8):
        dma("gpsimd", aWk[:, ic * D:(ic + 1) * D], I["aWoT"][ic * 128:(ic + 1) * 128, :])
    ds_wo = CH["Dg"].n
    w("tensor", CH["Dg"], ds_wo)
    for blk in range(NBO if PH >= 7 else 0):
        lnb = lnb0 if blk % 2 == 0 else lnb1
        w("sync", CH["Tm"])
        w("sync", CH["Dd0"])
        dma("sync", lnb.rearrange("p (c f) -> p c f", f=512),
            SC["oa"].rearrange("(c p) t -> p c t", p=128)[:, :, blk * 512:(blk + 1) * 512],
            CH["Dlnb"])
        w("tensor", CH["Dlnb"])
        w("tensor", CH["PSx"])
        for ic in range(8):
            mm(PS["psS"][0:8, 0:512], aA[:, (3 * 8 + ic) * 8:(3 * 8 + ic + 1) * 8],
               lnb[:, ic * 512:(ic + 1) * 512], ic == 0, ic == 7, CH["Tm"] if ic == 7 else None)
        w("vector", CH["Tm"])
        op("vector", "tensor_copy", ch=CH["PSx"], out=u4[0:8, 3 * 512:4 * 512],
           in_=PS["psS"][0:8, 0:512])
        w("tensor", CH["PSx"])
        for i in range(4):
            tl = blk * 4 + i
            w("sync", CH["Vm"])
            dma("sync", tt2[:, 0:D], I["x_own"][tl * 128:(tl + 1) * 128, :], CH["Dx"])
            for half in range(2):
                pb = PS["psA"] if half == 0 else PS["psB"]
                w("tensor", CH["PAB"], CH["PAB"].n - 1)
                for ic in range(8):
                    mm(pb[:, 0:512], lnb[:, ic * 512 + i * 128: ic * 512 + (i + 1) * 128],
                       aWk[:, ic * D + half * 512: ic * D + (half + 1) * 512], ic == 0, False)
                mm(pb[:, 0:512], u4[0:8, 3 * 512 + i * 128: 3 * 512 + (i + 1) * 128],
                   aB2[0:8, 3 * D + half * 512: 3 * D + (half + 1) * 512], False, False)
                mm(pb[:, 0:512], onesb[0:1, 0:128],
                   abias2[0:1, 3 * D + half * 512: 3 * D + (half + 1) * 512], False, True, CH["Tm"])
                w("vector", CH["Tm"])
                ser("vector", "tensor_scalar", out=ev[:, half * 512:(half + 1) * 512],
                    in0=pb[:, 0:512], scalar1=rz[:, 0:1], scalar2=None, op0=ALU.mult)
                w("vector", CH["VS"])
                op("vector", "engine_nop", ch=CH["PAB"])
            w("vector", CH["Dx"])
            c0 = 2 * tl
            ser("vector", "tensor_scalar", out=tt[:, 0:D], in0=tt2[:, 0:D],
                scalar1=lnstatow[:, c0:c0 + 1], scalar2=lnstatow[:, c0 + 1:c0 + 2],
                op0=ALU.add, op1=ALU.mult)
            ser("vector", "tensor_tensor", out=junk[:, 0:D], in0=tt2[:, 0:D], in1=tt[:, 0:D],
                op=ALU.add)
            ser("vector", "tensor_tensor", out=xres[:, tl * D:(tl + 1) * D],
                in0=junk[:, 0:D], in1=ev[:, 0:D], op=ALU.add)
            w("vector", CH["VS"])
            op("vector", "engine_nop", ch=CH["Vm"])


    GROUPS = [[0, 1], [2, 3], [4, 5], [6, 7]]
    wqf = av(0, 1024, F32)
    wqb = av(4, 1024)
    acc = stats

    def weight_quant_prep():
        if stage < 2:
            return
        wq_pieces = {"w1T": [(c, p) for c in range(8) for p in range(4)],
                     "w2T": [(c, 0) for c in range(32)]}
        for wi, wt in enumerate(("w1T", "w2T")):
            ser("vector", "memset", ap=stats[:, 26:27], constant=0.0)
            for (c, p) in wq_pieces[wt]:
                w("sync", CH["Vm"])
                dma("sync", wqf[:, 0:1024], I[wt][c * 128:(c + 1) * 128, p * 1024:(p + 1) * 1024],
                    CH["Dw"])
                w("vector", CH["Dw"])
                ser("vector", "reduce_sum", out=stats[:, 25:26], in_=wqf[:, 0:1024], axis=AX,
                    apply_absolute_value=True)
                ser("vector", "tensor_tensor", out=stats[:, 26:27], in0=stats[:, 26:27],
                    in1=stats[:, 25:26], op=ALU.add)
                w("vector", CH["VS"])
                op("vector", "engine_nop", ch=CH["Vm"])
            w("tensor", CH["Vm"])
            w("tensor", CH["PSx"])
            mm(PS["psS"][0:1, 0:1], stats[:, 26:27], ones64[:, 0:1], True, True, CH["Tm"])
            w("vector", CH["Tm"])
            ser("vector", "tensor_copy", out=rowb[0:1, 1600 + 3 * wi:1601 + 3 * wi],
                in_=PS["psS"][0:1, 0:1])
            w("vector", CH["VS"])
            op("vector", "engine_nop", ch=CH["PSx"])
            ser("vector", "tensor_scalar", out=rowb[0:1, 1620 + 4 * wi:1621 + 4 * wi],
                in0=rowb[0:1, 1600 + 3 * wi:1601 + 3 * wi],
                scalar1=1.0 / (DHID * D), scalar2=1e-5, op0=ALU.mult, op1=ALU.max)
            ser("vector", "tensor_copy", out=rowb[0:1, 1621 + 4 * wi:1622 + 4 * wi],
                in_=rowb[0:1, 1620 + 4 * wi:1621 + 4 * wi])
            ser("vector", "reciprocal", out=rowb[0:1, 1624 + 4 * wi:1626 + 4 * wi],
                in_=rowb[0:1, 1620 + 4 * wi:1622 + 4 * wi])
            w("vector", CH["VS"])
            op("vector", "engine_nop", ch=CH["Vm"])
            w("tensor", CH["Vm"])
            w("tensor", CH["PRx"])
            mm(PS["psR"][:, 0:1], onesf[0:1, 0:128], rowb[0:1, 1624 + 4 * wi:1625 + 4 * wi],
               True, True)
            mm(PS["psR"][:, 1:2], onesf[0:1, 0:128], rowb[0:1, 1620 + 4 * wi:1621 + 4 * wi],
               True, True, CH["Tm"])
            w("vector", CH["Tm"])
            ser("vector", "tensor_copy", out=wsc[:, 2 * wi:2 * wi + 2],
                in_=PS["psR"][:, 0:2])
            w("vector", CH["VS"])
            op("vector", "engine_nop", ch=CH["PRx"])
            op("vector", "engine_nop", ch=CH["Vm"])
        for wi, (wt, scn) in enumerate((("w1T", "w1q"), ("w2T", "w2q"))):
            wscol = wsc[:, 2 * wi:2 * wi + 1]
            for (c, p) in wq_pieces[wt]:
                w("sync", CH["Vm"])
                w("sync", CH["Dn"])
                dma("sync", wqf[:, 0:1024], I[wt][c * 128:(c + 1) * 128, p * 1024:(p + 1) * 1024],
                    CH["Dw"])
                w("vector", CH["Dw"])
                ser("vector", "tensor_scalar", out=junk[:, 0:1024], in0=wqf[:, 0:1024],
                    scalar1=wscol, scalar2=MAGIC, op0=ALU.mult, op1=ALU.add)
                ser("vector", "tensor_scalar", out=junk[:, 1024:2048],
                    in0=junk[:, 0:1024], scalar1=-MAGIC, scalar2=1.0, op0=ALU.add, op1=ALU.min)
                ser("vector", "tensor_scalar", out=wqb[:, 0:1024],
                    in0=junk[:, 1024:2048], scalar1=-1.0, scalar2=None, op0=ALU.max)
                w("vector", CH["VS"])
                op("vector", "engine_nop", ch=CH["Vm"])
                w("gpsimd", CH["Vm"])
                dma("gpsimd", SC[scn][c * 128:(c + 1) * 128, p * 1024:(p + 1) * 1024],
                    wqb[:, 0:1024], CH["Dn"])

    weight_quant_prep()

    n1t = av(16, 4 * D)
    n1fm = av(24, 8 * 512)
    wstr = av(32, 4096)
    hbuf = av(40, 2 * 512, F32)
    hin = av(44, 2 * 514, F32)
    h2s = av(48.5, 512, F32)
    runmax = av(50.5, 512, F32)
    runmaxB = av(58, 512, F32)
    n2fm = av(52.5, 4 * 512)
    halb = av(56.5, 320, F32)

    def ffn_pass(tag, write_ag2):
        agi, ago = (SC["ag1i"], SC["ag1o"]) if tag == 1 else (SC["ag4i"], SC["ag4o"])
        for blk in range(NBO):
            for g in range(2):
                t0 = blk * 4 + g * 2
                ln_block_sb(lambda i, t_=t0: xres[:, (t_ + i) * D:(t_ + i + 1) * D],
                            lnstatow, t0, 2, lnt)
                for i in range(2):
                    ser("vector", "reduce_max", out=stats[:, 0:1],
                        in_=lnt[:, i * D:(i + 1) * D], axis=AX, apply_absolute_value=True)
                    ser("vector", "tensor_scalar", out=stats[:, 1:2], in0=stats[:, 0:1],
                        scalar1=1e-5, scalar2=127.0, op0=ALU.max, op1=ALU.bypass)
                    ser("vector", "tensor_copy", out=stats[:, 2:3], in_=stats[:, 1:2])
                    ser("vector", "reciprocal", out=stats[:, 4:6], in_=stats[:, 1:3])
                    ser("vector", "tensor_scalar", out=stats[:, 6:7], in0=stats[:, 4:5],
                        scalar1=127.0, scalar2=None, op0=ALU.mult)
                    ser("vector", "scalar_tensor_tensor", out=stats[:, 7:8],
                        in0=stats[:, 1:2], scalar=1.0 / 127.0, in1=wsc[:, 1:2],
                        op0=ALU.mult, op1=ALU.mult)
                    ser("vector", "tensor_scalar", out=junk[:, 0:D],
                        in0=lnt[:, i * D:(i + 1) * D], scalar1=stats[:, 6:7], scalar2=MAGIC,
                        op0=ALU.mult, op1=ALU.add)
                    ser("vector", "tensor_scalar", out=junk[:, D:2 * D], in0=junk[:, 0:D],
                        scalar1=-MAGIC, scalar2=None, op0=ALU.add)
                    ser("vector", "tensor_scalar", out=n1t[:, (g * 2 + i) * D:(g * 2 + i + 1) * D],
                        in0=junk[:, D:2 * D], scalar1=stats[:, 7:8], scalar2=None, op0=ALU.mult)
                w("vector", CH["VS"])
                op("vector", "engine_nop", ch=CH["Vm"])
                transpose_to_sbuf(n1t, g * 2, 2, n1fm, 512, g * 256)
            w("tensor", CH["PTx"])
            for grp in range(8):
                w("sync", CH["Tm"])
                dma("sync", wstr.rearrange("p (a f) -> p a f", f=512),
                    SC["w1q"].rearrange("(a p) h -> p a h", p=128)[:, :, grp * 512:(grp + 1) * 512],
                    CH["Dw"])
                w("tensor", CH["Dw"])
                for oc4 in range(4):
                    oc = grp * 4 + oc4
                    pb = PS["psA"] if oc % 2 == 0 else PS["psB"]
                    w("tensor", CH["PAB"], CH["PAB"].n - 1)
                    for ic in range(8):
                        mm(pb[:, 0:512], wstr[:, ic * 512 + oc4 * 128: ic * 512 + (oc4 + 1) * 128],
                           n1fm[:, ic * 512:(ic + 1) * 512], ic == 0, ic == 7, CH["Tm"] if ic == 7 else None)
                    sl = oc % 2
                    w("vector", CH["Tm"])
                    w("vector", CH["Dh0" if sl == 0 else "Dh1"])
                    ser("vector", "tensor_copy", out=hbuf[:, sl * 512:(sl + 1) * 512],
                        in_=pb[:, 0:512])
                    w("vector", CH["VS"])
                    op("vector", "engine_nop", ch=CH["PAB"])
                    if blk == 0:
                        ser("vector", "tensor_copy", out=halb[:, oc:oc + 1],
                            in_=hbuf[:, sl * 512:sl * 512 + 1])
                    if blk == NBO - 1:
                        ser("vector", "tensor_copy", out=halb[:, 32 + oc:33 + oc],
                            in_=hbuf[:, sl * 512 + 511:sl * 512 + 512])
                    w("vector", CH["VS"])
                    op("vector", "engine_nop", ch=CH["Vm"])
                    w("gpsimd", CH["Vm"])
                    dma("gpsimd", SC["h"][oc, :, blk * 512:(blk + 1) * 512],
                        hbuf[:, sl * 512:(sl + 1) * 512], CH["Dh0" if sl == 0 else "Dh1"])
        w("gpsimd", CH["Vm"])
        dma("gpsimd", agi, halb[:, 0:64], CH["Dn"])
        w("gpsimd", CH["Dn"])
        op("gpsimd", "collective_compute", ch=CH["AG"], kind="AllGather",
           op=ALU.bypass, replica_groups=GROUPS, ins=[agi.opt()], outs=[ago.opt()])
        w("sync", CH["AG"])
        w("sync", CH["Vm"])
        dma("sync", halb[:, 64:192].rearrange("p (r e c) -> p r e c", r=2, e=2),
            ago.rearrange("r p (e c) -> p r e c", e=2), CH["Dhr"])
        w("vector", CH["Dhr"])
        ser("vector", "tensor_scalar", out=halb[:, 224:256], in0=halb[:, 160:192],
            scalar1=selhalo[:, 1:2], scalar2=None, op0=ALU.mult)
        ser("vector", "scalar_tensor_tensor", out=halb[:, 192:224], in0=halb[:, 96:128],
            scalar=selhalo[:, 0:1], in1=halb[:, 224:256], op0=ALU.mult, op1=ALU.add)
        ser("vector", "tensor_scalar", out=halb[:, 256:288], in0=halb[:, 64:96],
            scalar1=selhalo[:, 2:3], scalar2=None, op0=ALU.mult)
        ser("vector", "scalar_tensor_tensor", out=halb[:, 224:256], in0=halb[:, 128:160],
            scalar=selhalo[:, 3:4], in1=halb[:, 256:288], op0=ALU.mult, op1=ALU.add)
        w("vector", CH["VS"])
        op("vector", "engine_nop", ch=CH["Vm"])
        w("sync", CH["Dh0"])
        w("sync", CH["Dh1"])
        for blk in (1, 2, 0, 3):
            w("vector", CH["Tm"])
            ser("vector", "memset", ap=runmax[:, 0:512], constant=0.0)
            ser("vector", "memset", ap=runmaxB[:, 0:512], constant=0.0)
            for oc in range(32):
                w("sync", CH["Vm"])
                l0 = blk * 512
                if blk == 0:
                    dma("sync", hin[:, 1:514], SC["h"][oc, :, 0:513], CH["Dhr"])
                elif blk == NBO - 1:
                    dma("sync", hin[:, 0:513], SC["h"][oc, :, l0 - 1:l0 + 512], CH["Dhr"])
                else:
                    dma("sync", hin[:, 0:514], SC["h"][oc, :, l0 - 1:l0 + 513], CH["Dhr"])
                w("vector", CH["Dhr"])
                if blk == 0:
                    ser("vector", "tensor_copy", out=hin[:, 0:1], in_=halb[:, 192 + oc:193 + oc])
                if blk == NBO - 1:
                    ser("vector", "tensor_copy", out=hin[:, 513:514], in_=halb[:, 224 + oc:225 + oc])
                w("vector", CH["Dh0"])
                ser("vector", "tensor_scalar", out=h2s[:, 0:512], in0=hin[:, 0:512],
                    scalar1=convw[:, 0 * 32 + oc:0 * 32 + oc + 1],
                    scalar2=None, op0=ALU.mult)
                ser("vector", "scalar_tensor_tensor", out=h2s[:, 0:512], in0=hin[:, 1:513],
                    scalar=convw[:, 1 * 32 + oc:1 * 32 + oc + 1], in1=h2s[:, 0:512],
                    op0=ALU.mult, op1=ALU.add)
                ser("vector", "scalar_tensor_tensor", out=h2s[:, 0:512], in0=hin[:, 2:514],
                    scalar=convw[:, 2 * 32 + oc:2 * 32 + oc + 1], in1=h2s[:, 0:512],
                    op0=ALU.mult, op1=ALU.add)
                ser("vector", "tensor_scalar", out=hin[:, 514:1026], in0=h2s[:, 0:512],
                    scalar1=convb[:, oc:oc + 1], scalar2=None, op0=ALU.add)
                w("vector", CH["VS"])
                op("vector", "engine_nop", ch=CH["Vm"])
                w("scalar", CH["Vm"])
                w("scalar", CH["Dh0"])
                ser("scalar", "activation", out=h2s[:, 0:512], in_=hin[:, 514:1026], func=AF.Silu)
                ser("scalar", "activation", out=junk[:, 1024:1536], in_=h2s[:, 0:512], func=AF.Abs)
                w("vector", CH["SS"])
                rsrc, rdst = (runmax, runmaxB) if oc % 2 == 0 else (runmaxB, runmax)
                ser("vector", "tensor_tensor", out=rdst[:, 0:512], in0=rsrc[:, 0:512],
                    in1=junk[:, 1024:1536], op=ALU.max)
                if blk == 1 and oc == 2 and "dbg_rm" in dbg:
                    ser("vector", "tensor_copy", out=lnt[:, 0:512], in_=rdst[:, 0:512])
                    ser("vector", "tensor_copy", out=lnt[:, 512:1024], in_=h2s[:, 0:512])
                    ser("vector", "tensor_copy", out=lnt[:, 1024:1536], in_=hin[:, 1:513])
                    ser("vector", "tensor_copy", out=lnt[:, 1536:2048], in_=hin[:, 514:1026])
                w("vector", CH["VS"])
                op("vector", "engine_nop", ch=CH["Vm"])
                w("gpsimd", CH["Vm"])
                dma("gpsimd", SC["h2"][oc, :, l0:l0 + 512], h2s[:, 0:512], CH["Dh0"])
            if blk == 1 and "dbg_rm" in dbg:
                w("gpsimd", CH["Vm"])
                dma("gpsimd", OUT["dbg_rm"], runmax[:, 0:512], CH["Dd"])
                dma("gpsimd", OUT["dbg_hin"], hin[:, 0:1028], CH["Dd"])
                dma("gpsimd", OUT["dbg_h2s"], h2s[:, 0:512], CH["Dd"])
                w("vector", CH["Dd"])
                w("sync", CH["Dd"])
            for r in range(4):
                bank = PS["psT"] if r % 2 == 0 else PS["psU"]
                w("tensor", CH["Vm"])
                w("tensor", CH["PTx"], CH["PTx"].n - 1)
                mm(bank[:, 0:128], runmax[:, r * 128:(r + 1) * 128], identf[:], True, True, CH["Tm"])
                w("vector", CH["Tm"])
                ser("vector", "reduce_max", out=stats[:, 8 + r:9 + r], in_=bank[:, 0:128], axis=AX)
                w("vector", CH["VS"])
                op("vector", "engine_nop", ch=CH["PTx"])
            for r in range(4):
                w("tensor", CH["VS"])
                w("tensor", CH["PRx"])
                mm(PS["psR"][0:1, r * 128:(r + 1) * 128], stats[:, 8 + r:9 + r], identf[:],
                   True, True, CH["Tm"])
            w("vector", CH["Tm"])
            ser("vector", "tensor_scalar", out=rowb[0:1, 1024:1536], in0=PS["psR"][0:1, 0:512],
                scalar1=1e-5, scalar2=None, op0=ALU.max)
            w("vector", CH["VS"])
            op("vector", "engine_nop", ch=CH["PRx"])
            ser("vector", "reciprocal", out=rowb2[0:1, 1024:1536], in_=rowb[0:1, 1024:1536])
            ser("vector", "tensor_scalar", out=rowb2[0:1, 0:512], in0=rowb2[0:1, 1024:1536],
                scalar1=127.0, scalar2=None, op0=ALU.mult)
            ser("vector", "tensor_scalar", out=rowb[0:1, 0:512], in0=rowb[0:1, 1024:1536],
                scalar1=wsc[0:1, 3:4], scalar2=1.0 / 127.0, op0=ALU.mult, op1=ALU.mult)
            w("vector", CH["VS"])
            op("vector", "engine_nop", ch=CH["Vm"])
            w("tensor", CH["Vm"])
            w("tensor", CH["PRx"])
            mm(PS["psR"][:, 0:512], onesf[0:1, 0:128], rowb2[0:1, 0:512], True, True)
            mm(PS["psX"][:, 0:512], onesf[0:1, 0:128], rowb[0:1, 0:512], True, True, CH["Tm"])
            w("vector", CH["Tm"])
            ser("vector", "tensor_copy", out=rrep[:, 0:512], in_=PS["psR"][:, 0:512])
            ser("vector", "tensor_copy", out=ev[:, 0:512], in_=PS["psX"][:, 0:512])
            w("vector", CH["VS"])
            op("vector", "engine_nop", ch=CH["PRx"])
            op("vector", "engine_nop", ch=CH["Vm"])
            for chnm in ("PAB", "PSx", "POx", "PTx"):
                w("tensor", CH[chnm])
            banks = [PS[nm] for nm in ("psA", "psB", "psO", "psS", "psR", "psT", "psU", "psX")]
            w("tensor", CH["PRx"])
            for grp in range(8):
                w("sync", CH["Tm"])
                dma("sync", wstr.rearrange("p (a f) -> p a f", f=1024),
                    SC["w2q"].rearrange("(a p) o -> p a o", p=128)[:, grp * 4:(grp + 1) * 4, :],
                    CH["Dw"])
                for c4 in range(4):
                    oc = grp * 4 + c4
                    w("sync", CH["Vm"])
                    w("sync", CH["Dh0"])
                    dma("sync", hin[:, 0:512],
                        SC["h2"][oc, :, blk * 512:(blk + 1) * 512], CH["Dhr"])
                    w("vector", CH["Dhr"])
                    sl = oc % 4
                    w("vector", CH["Tm"], CH["Tm"].n - 3)
                    ser("vector", "tensor_tensor", out=junk[:, 0:512], in0=hin[:, 0:512],
                        in1=rrep[:, 0:512], op=ALU.mult)
                    ser("vector", "tensor_scalar", out=junk[:, 512:1024], in0=junk[:, 0:512],
                        scalar1=MAGIC, scalar2=-MAGIC, op0=ALU.add, op1=ALU.add)
                    ser("vector", "tensor_tensor", out=n2fm[:, sl * 512:(sl + 1) * 512],
                        in0=junk[:, 512:1024], in1=ev[:, 0:512], op=ALU.mult)
                    w("vector", CH["VS"])
                    op("vector", "engine_nop", ch=CH["Vm"])
                    w("tensor", CH["Vm"])
                    w("tensor", CH["Dw"])
                    for tile in range(4):
                        for half in range(2):
                            mm(banks[tile * 2 + half][:, 0:512],
                               n2fm[:, sl * 512 + tile * 128: sl * 512 + (tile + 1) * 128],
                               wstr[:, c4 * 1024 + half * 512: c4 * 1024 + (half + 1) * 512],
                               oc == 0, oc == 31, CH["Tm"] if (tile == 3 and half == 1) else None)
            w("vector", CH["Tm"])
            for tile in range(4):
                tl = blk * 4 + tile
                for half in range(2):
                    ser("vector", "tensor_tensor", out=tt[:, half * 512:(half + 1) * 512],
                        in0=banks[tile * 2 + half][:, 0:512],
                        in1=xres[:, tl * D + half * 512: tl * D + (half + 1) * 512], op=ALU.add)
                ser("vector", "tensor_copy", out=xres[:, tl * D:(tl + 1) * D], in_=tt[:, 0:D])
                w("vector", CH["VS"])
                op("vector", "engine_nop", ch=CH["Vm"])
                if write_ag2:
                    w("gpsimd", CH["Vm"])
                    dma("gpsimd", SC["ag2i"][tl * 128:(tl + 1) * 128, :],
                        xres[:, tl * D:(tl + 1) * D], CH["Dn"])
            w("vector", CH["VS"])
            op("vector", "engine_nop", ch=CH["POx"])
            w("tensor", CH["POx"])

    I_rows = {"smear": None, "lsc": None}
    smearrow = sb("smearrow", [1, 16])
    dma("gpsimd", smearrow[0:1, 0:8], I["b_smear_row"])
    dma("gpsimd", smearrow[0:1, 8:16], I["b_lsc_row"])
    I_rows["smear"] = smearrow[0:1, 0:8]
    I_rows["lsc"] = smearrow[0:1, 8:16]

    CLIST = [(qc, kt) for qc in range(8) for kt in range(4 * (qc + 1))]

    def band_idx(qc, kt):
        lo = 4 * (qc + 1) - 6
        if kt >= lo:
            return qc * 6 + (kt - lo)
        return 48

    def attnb_pass():
        w("sync", CH["AG"])
        w("gpsimd", CH["AG"])
        w("scalar", CH["Ds"])
        w("scalar", CH["Dg"])
        ser("scalar", "activation", out=rowb[0:1, 1700:1708], in_=I_rows["smear"], func=AF.Sigmoid)
        ser("scalar", "activation", out=rowb[0:1, 1708:1716], in_=I_rows["lsc"], func=AF.Exp)
        w("vector", CH["SS"])
        ser("vector", "reciprocal", out=rowb[0:1, 1716:1724], in_=rowb[0:1, 1708:1716])
        ser("vector", "tensor_scalar", out=rowb[0:1, 1724:1732], in0=rowb[0:1, 1700:1708],
            scalar1=-1.0, scalar2=1.0, op0=ALU.mult, op1=ALU.add)
        ser("vector", "tensor_tensor", out=rowb[0:1, 1732:1740], in0=rowb[0:1, 1724:1732],
            in1=rowb[0:1, 1716:1724], op=ALU.mult)
        ser("vector", "tensor_tensor", out=rowb[0:1, 1740:1748], in0=rowb[0:1, 1700:1708],
            in1=rowb[0:1, 1716:1724], op=ALU.mult)
        w("vector", CH["VS"])
        op("vector", "engine_nop", ch=CH["Vm"])
        w("tensor", CH["Vm"])
        w("tensor", CH["PRx"])
        mm(PS["psR"][:, 0:8], onesf[0:1, 0:128], rowb[0:1, 1732:1740], True, True)
        mm(PS["psR"][:, 8:16], onesf[0:1, 0:128], rowb[0:1, 1740:1748], True, True)
        mm(PS["psR"][:, 16:24], onesf[0:1, 0:128], rowb[0:1, 1716:1724], True, True, CH["Tm"])
        w("vector", CH["Tm"])
        ser("vector", "tensor_copy", out=smearc[:, 0:24], in_=PS["psR"][:, 0:24])
        w("vector", CH["VS"])
        op("vector", "engine_nop", ch=CH["PRx"])
        op("vector", "engine_nop", ch=CH["Vm"])
        for sl in range(8):
            w("sync", CH["Vm"])
            dma("sync", rowb[0:1, 512:1024], I["qpre"][sl:sl + 1, :], CH["Dx"])
            w("vector", CH["Dx"])
            w("vector", CH["Dn"])
            ser("vector", "tensor_scalar", out=rowb[0:1, 0:512], in0=rowb[0:1, 512:1024],
                scalar1=smearc[0:1, 16 + sl:17 + sl], scalar2=None, op0=ALU.mult)
            w("vector", CH["VS"])
            op("vector", "engine_nop", ch=CH["Vm"])
            w("gpsimd", CH["Vm"])
            dma("gpsimd", SC["qps"][sl:sl + 1, :], rowb[0:1, 0:512], CH["Dn"])

        for blk in range(NBF):
            for g in range(2):
                t0 = blk * 4 + g * 2
                ln_block(lambda i, t_=t0: SC["ag2o"][(t_ + i) * 128:(t_ + i + 1) * 128, :],
                         lnstat, t0, 2, xblk, lnt)
                transpose_block_to_scratch(lnt, 2, SC["x2lnf"], t0 * 128)
        for blk in range(NBO):
            for g in range(2):
                t0 = blk * 4 + g * 2
                ln_block_sb(lambda i, t_=t0: xres[:, (t_ + i) * D:(t_ + i + 1) * D],
                            lnstatow, t0, 2, lnt)
                transpose_block_to_scratch(lnt, 2, SC["x2lno"], t0 * 128)

        aWk4 = av(0, 8 * D)
        lnq = av(16, 8 * 512)
        eqb = av(24, 2 * 512, F32)
        def qk_pass(which):
            wname = "bWq_my" if which == "q" else "bWk_my"
            lnwcol = 8 if which == "q" else 16
            dstsc = SC["qb"] if which == "q" else SC["kb"]
            biasoff = 0 if which == "q" else 1024
            for ic in range(8):
                dma("gpsimd", aWk4[:, ic * D:(ic + 1) * D], I[wname][ic * 128:(ic + 1) * 128, :])
            w("tensor", CH["Dg"])
            if which == "k":
                ser("vector", "memset", ap=SB["kcarry"][:], constant=0.0)
            for blk in range(NBF):
                lnb = lnb0 if blk % 2 == 0 else lnb1
                w("sync", CH["Tm"])
                w("sync", CH["Dd"])
                dma("sync", lnb.rearrange("p (c f) -> p c f", f=512),
                    SC["x2lnf"].rearrange("(c p) t -> p c t", p=128)[:, :, blk * 512:(blk + 1) * 512],
                    CH["Dlnb"])
                w("vector", CH["Dlnb"])
                w("vector", CH["Tm"])
                for ic in range(8):
                    ser("vector", "tensor_scalar", out=lnq[:, ic * 512:(ic + 1) * 512],
                        in0=lnb[:, ic * 512:(ic + 1) * 512],
                        scalar1=blw[:, lnwcol + ic:lnwcol + ic + 1],
                        scalar2=blb[:, lnwcol + ic:lnwcol + ic + 1],
                        op0=ALU.mult, op1=ALU.add)
                w("vector", CH["VS"])
                op("vector", "engine_nop", ch=CH["Vm"])
                w("tensor", CH["Vm"])
                for oc in range(8):
                    if which == "q":
                        w("sync", CH["Vm"])
                        dma("sync", rowb[0:1, 1024:1536], SC["qps"][oc:oc + 1, :], CH["Dx"])
                    pb = PS["psA"] if oc % 2 == 0 else PS["psB"]
                    w("tensor", CH["PAB"], CH["PAB"].n - 1)
                    for ic in range(8):
                        mm(pb[:, 0:512], aWk4[:, ic * D + oc * 128: ic * D + (oc + 1) * 128],
                           lnq[:, ic * 512:(ic + 1) * 512], ic == 0, False)
                    mm(pb[:, 0:512], bqkb[0:1, biasoff + oc * 128: biasoff + (oc + 1) * 128],
                       onesb[0:1, 0:512], False, True, CH["Tm"])
                    w("scalar", CH["Tm"])
                    w("scalar", CH["Vm"])
                    sl = oc % 2
                    op("scalar", "activation", ch=CH["PAB"], out=eqb[:, sl * 512:(sl + 1) * 512],
                       in_=pb[:, 0:512], func=AF.Exp, bias=neg16[:, 0:1], scale=1.0)
                    w("tensor", CH["PAB"])
                    w("tensor", CH["PSx"])
                    mm(PS["psS"][0:1, 0:512], ones64[:, 0:1],
                       eqb[:, sl * 512:(sl + 1) * 512], True, True, CH["Tm"])
                    w("vector", CH["Tm"])
                    ser("vector", "tensor_copy", out=rowb[0:1, 0:512], in_=PS["psS"][0:1, 0:512])
                    w("vector", CH["VS"])
                    op("vector", "engine_nop", ch=CH["PSx"])
                    ser("vector", "reciprocal", out=rowb2[0:1, 0:512], in_=rowb[0:1, 0:512])
                    if which == "q":
                        w("vector", CH["Dx"])
                        ser("vector", "tensor_tensor", out=rowb[0:1, 0:512],
                            in0=rowb2[0:1, 0:512], in1=rowb[0:1, 1024:1536], op=ALU.mult)
                        rsrc_row = rowb[0:1, 0:512]
                    else:
                        rsrc_row = rowb2[0:1, 0:512]
                    w("vector", CH["VS"])
                    op("vector", "engine_nop", ch=CH["Vm"])
                    w("tensor", CH["Vm"])
                    w("tensor", CH["PRx"])
                    mm(PS["psR"][:, 0:512], onesf[0:1, 0:128], rsrc_row, True, True, CH["Tm"])
                    w("vector", CH["Tm"])
                    ser("vector", "tensor_copy", out=rrep[:, 0:512], in_=PS["psR"][:, 0:512])
                    w("vector", CH["VS"])
                    op("vector", "engine_nop", ch=CH["PRx"])
                    if which == "q":
                        w("vector", CH["Dd0"])
                        ser("vector", "tensor_tensor", out=evb[:, 0:512],
                            in0=eqb[:, sl * 512:(sl + 1) * 512], in1=rrep[:, 0:512], op=ALU.mult)
                        w("vector", CH["VS"])
                        op("vector", "engine_nop", ch=CH["Vm"])
                        w("gpsimd", CH["Vm"])
                        dma("gpsimd", dstsc[oc * 128:(oc + 1) * 128, blk * 512:(blk + 1) * 512],
                            evb[:, 0:512], CH["Dd0"])
                    else:
                        ser("vector", "tensor_tensor", out=SB["ktmp"][:, 0:512],
                            in0=eqb[:, sl * 512:(sl + 1) * 512], in1=rrep[:, 0:512], op=ALU.mult)
                        ser("vector", "tensor_scalar", out=SB["ktmp2"][:, 1:512],
                            in0=SB["ktmp"][:, 0:511],
                            scalar1=smearc[:, 8 + oc:9 + oc], scalar2=None, op0=ALU.mult)
                        ser("vector", "tensor_scalar", out=SB["ktmp2"][:, 0:1],
                            in0=SB["kcarry"][:, oc:oc + 1],
                            scalar1=smearc[:, 8 + oc:9 + oc], scalar2=None, op0=ALU.mult)
                        ser("vector", "tensor_copy", out=SB["kcarry"][:, oc:oc + 1],
                            in_=SB["ktmp"][:, 511:512])
                        w("vector", CH["Dd0"])
                        ser("vector", "scalar_tensor_tensor", out=evb[:, 0:512],
                            in0=SB["ktmp"][:, 0:512], scalar=smearc[:, 0 + oc:1 + oc],
                            in1=SB["ktmp2"][:, 0:512], op0=ALU.mult, op1=ALU.add)
                        w("vector", CH["VS"])
                        op("vector", "engine_nop", ch=CH["Vm"])
                        w("gpsimd", CH["Vm"])
                        dma("gpsimd", dstsc[oc * 128:(oc + 1) * 128, blk * 512:(blk + 1) * 512],
                            evb[:, 0:512], CH["Dd0"])

        w("sync", CH["Dn"])
        qk_pass("q")
        qk_pass("k")

        for ic in range(8):
            dma("gpsimd", aWk4[:, ic * D:(ic + 1) * D], I["bWin_v"][ic * 128:(ic + 1) * 128, :])
        w("tensor", CH["Dg"])
        for blk in range(NBF):
            lnb = lnb0 if blk % 2 == 0 else lnb1
            w("sync", CH["Tm"])
            dma("sync", lnb.rearrange("p (c f) -> p c f", f=512),
                SC["x2lnf"].rearrange("(c p) t -> p c t", p=128)[:, :, blk * 512:(blk + 1) * 512],
                CH["Dlnb"])
            w("vector", CH["Dlnb"])
            w("vector", CH["Tm"])
            for ic in range(8):
                ser("vector", "tensor_scalar", out=lnq[:, ic * 512:(ic + 1) * 512],
                    in0=lnb[:, ic * 512:(ic + 1) * 512],
                    scalar1=blw[:, 0 + ic:1 + ic], scalar2=blb[:, 0 + ic:1 + ic],
                    op0=ALU.mult, op1=ALU.add)
            w("vector", CH["VS"])
            op("vector", "engine_nop", ch=CH["Vm"])
            w("tensor", CH["Vm"])
            for i in range(4):
                for half in range(2):
                    pb = PS["psA"] if half == 0 else PS["psB"]
                    w("tensor", CH["PAB"], CH["PAB"].n - 1)
                    for ic in range(8):
                        mm(pb[:, 0:512], lnq[:, ic * 512 + i * 128: ic * 512 + (i + 1) * 128],
                           aWk4[:, ic * D + half * 512: ic * D + (half + 1) * 512], ic == 0, ic == 7,
                           CH["Tm"] if ic == 7 else None)
                    sl = half
                    w("scalar", CH["Tm"])
                    w("scalar", CH["Dd0" if sl == 0 else "Dd1"])
                    op("scalar", "activation", ch=CH["PAB"], out=evb[:, sl * 512:(sl + 1) * 512],
                       in_=pb[:, 0:512], func=AF.Identity)
                    w("gpsimd", CH["PAB"])
                    dma("gpsimd", SC["vb"][(blk * 4 + i) * 128:(blk * 4 + i + 1) * 128,
                                           half * 512:(half + 1) * 512],
                        evb[:, sl * 512:(sl + 1) * 512], CH["Dd0" if sl == 0 else "Dd1"])

        for oh in range(2):
            w("gpsimd", CH["Tm"])
            for ic in range(8):
                dma("gpsimd", aWk4[:, ic * D:(ic + 1) * D],
                    I["bWin_p"][ic * 128:(ic + 1) * 128, oh * 1024:(oh + 1) * 1024])
            w("tensor", CH["Dg"])
            for blk in range(NBO):
                lnb = lnb0 if blk % 2 == 0 else lnb1
                w("sync", CH["Tm"])
                dma("sync", lnb.rearrange("p (c f) -> p c f", f=512),
                    SC["x2lno"].rearrange("(c p) t -> p c t", p=128)[:, :, blk * 512:(blk + 1) * 512],
                    CH["Dlnb"])
                w("vector", CH["Dlnb"])
                w("vector", CH["Tm"])
                for ic in range(8):
                    ser("vector", "tensor_scalar", out=lnq[:, ic * 512:(ic + 1) * 512],
                        in0=lnb[:, ic * 512:(ic + 1) * 512],
                        scalar1=blw[:, 0 + ic:1 + ic], scalar2=blb[:, 0 + ic:1 + ic],
                        op0=ALU.mult, op1=ALU.add)
                w("vector", CH["VS"])
                op("vector", "engine_nop", ch=CH["Vm"])
                w("tensor", CH["Vm"])
                for oc8 in range(8):
                    oc = oh * 8 + oc8
                    pb = PS["psA"] if oc8 % 2 == 0 else PS["psB"]
                    w("tensor", CH["PAB"], CH["PAB"].n - 1)
                    for ic in range(8):
                        mm(pb[:, 0:512], aWk4[:, ic * D + oc8 * 128: ic * D + (oc8 + 1) * 128],
                           lnq[:, ic * 512:(ic + 1) * 512], ic == 0, ic == 7,
                           CH["Tm"] if ic == 7 else None)
                    sl = oc8 % 2
                    w("scalar", CH["Tm"])
                    w("scalar", CH["Dd0" if sl == 0 else "Dd1"])
                    op("scalar", "activation", ch=CH["PAB"], out=evb[:, sl * 512:(sl + 1) * 512],
                       in_=pb[:, 0:512], func=AF.Identity)
                    w("gpsimd", CH["PAB"])
                    dma("gpsimd", SC["pb"][oc * 128:(oc + 1) * 128, blk * 512:(blk + 1) * 512],
                        evb[:, sl * 512:(sl + 1) * 512], CH["Dd0" if sl == 0 else "Dd1"])

        trilm = av(20, 5 * 512, F32)
        w("sync", CH["Tm"])
        dma("sync", trilm, I["trilm"], CH["Ds"])
        w("vector", CH["Ds"])
        kslab = av(0, S)
        vslab = av(8, 32 * 128)
        qchunk = av(16, 512)
        btbuf = av(17, 2 * 512)
        btfill = av(19, 512)
        for slot in range(8):
            w("sync", CH["Tm"])
            w("sync", CH["Dd0"])
            w("sync", CH["Dd1"])
            dma("sync", kslab[:, 0:S], SC["kb"][slot * 128:(slot + 1) * 128, :], CH["Dkvq"])
            dma("sync", vslab[:, 0:32 * 128].rearrange("p (t c) -> p t c", c=128),
                SC["vb"][:, slot * 128:(slot + 1) * 128].rearrange("(t p) c -> p t c", p=128),
                CH["Dkvq"])
            if slot < 3:
                dma("sync", btfill[:, 0:512], I["btile"][(slot * 49 + 48) * 128:(slot * 49 + 49) * 128, :],
                    CH["Dkvq"])
            for qc in range(8):
                w("sync", CH["Tm"])
                dma("sync", qchunk[:, 0:512], SC["qb"][slot * 128:(slot + 1) * 128,
                                                       qc * 512:(qc + 1) * 512], CH["Dkvq"])
                w("tensor", CH["Dkvq"])
                w("tensor", CH["POx"])
                w("tensor", CH["PSx"])
                nkt = 4 * (qc + 1)
                for kt in range(nkt):
                    pb = PS["psA"] if kt % 2 == 0 else PS["psB"]
                    w("tensor", CH["PEXP"], CH["PEXP"].n - 1)
                    mm(pb[:, 0:512], kslab[:, kt * 128:(kt + 1) * 128], qchunk[:, 0:512],
                       True, True, CH["Tm"])
                    if slot < 3:
                        bi = band_idx(qc, kt)
                        if bi < 48:
                            w("sync", CH["PEXP"], CH["PEXP"].n - 1)
                            bch = CH["Dbt"] if kt % 2 == 0 else CH["Dbt1"]
                            dma("sync", btbuf[:, (kt % 2) * 512:(kt % 2 + 1) * 512],
                                I["btile"][(slot * 49 + bi) * 128:(slot * 49 + bi + 1) * 128, :],
                                bch)
                            tile2 = btbuf[:, (kt % 2) * 512:(kt % 2 + 1) * 512]
                            w("vector", bch)
                        else:
                            tile2 = btfill[:, 0:512]
                    else:
                        if kt >= 4 * qc:
                            tile2 = trilm[:, (kt - 4 * qc) * 512:(kt - 4 * qc + 1) * 512]
                        else:
                            tile2 = trilm[:, 4 * 512:5 * 512]
                    w("vector", CH["Tm"])
                    ser("vector", "scalar_tensor_tensor",
                        out=ptile[:, (kt % 2) * 512:(kt % 2 + 1) * 512],
                        in0=pb[:, 0:512],
                        scalar=btab[:, slot * 144 + CLIST.index((qc, kt)):
                                    slot * 144 + CLIST.index((qc, kt)) + 1],
                        in1=tile2, op0=ALU.mult, op1=ALU.mult)
                    w("vector", CH["VS"])
                    op("vector", "engine_nop", ch=CH["PEXP"])
                    w("tensor", CH["PEXP"])
                    mm(PS["psO"][:, 0:512], vslab[:, kt * 128:(kt + 1) * 128],
                       ptile[:, (kt % 2) * 512:(kt % 2 + 1) * 512], kt == 0, kt == nkt - 1)
                    mm(PS["psS"][0:1, 0:512], onescb[:, 0:1],
                       ptile[:, (kt % 2) * 512:(kt % 2 + 1) * 512], kt == 0, kt == nkt - 1,
                       CH["TAV"] if kt == nkt - 1 else None)
                w("vector", CH["TAV"])
                ser("vector", "tensor_copy", out=Osb[:, 0:512], in_=PS["psO"][:, 0:512])
                w("vector", CH["VS"])
                op("vector", "engine_nop", ch=CH["POx"])
                ser("vector", "tensor_scalar", out=rowb[0:1, 0:512], in0=PS["psS"][0:1, 0:512],
                    scalar1=EPS, scalar2=None, op0=ALU.add)
                w("vector", CH["VS"])
                op("vector", "engine_nop", ch=CH["PSx"])
                ser("vector", "reciprocal", out=rowb2[0:1, 0:512], in_=rowb[0:1, 0:512])
                w("vector", CH["VS"])
                op("vector", "engine_nop", ch=CH["Vm"])
                w("tensor", CH["Vm"])
                w("tensor", CH["PRx"])
                mm(PS["psR"][:, 0:512], onesf[0:1, 0:128], rowb2[0:1, 0:512], True, True, CH["Tm"])
                w("vector", CH["Tm"])
                ser("vector", "tensor_copy", out=rrep[:, 0:512], in_=PS["psR"][:, 0:512])
                w("vector", CH["VS"])
                op("vector", "engine_nop", ch=CH["PRx"])
                w("vector", CH["Dd0"])
                ser("vector", "tensor_tensor", out=evb[:, 0:512], in0=Osb[:, 0:512],
                    in1=rrep[:, 0:512], op=ALU.mult)
                w("vector", CH["VS"])
                op("vector", "engine_nop", ch=CH["Vm"])
                w("gpsimd", CH["Vm"])
                dma("gpsimd", SC["ag3i"][slot * 128:(slot + 1) * 128, qc * 512:(qc + 1) * 512],
                    evb[:, 0:512], CH["Dd0"])

        w("gpsimd", CH["Dd0"])
        op("gpsimd", "collective_compute", ch=CH["AG"], kind="AllGather",
           op=ALU.bypass, replica_groups=GROUPS, ins=[SC["ag3i"].opt()], outs=[SC["ag3o"].opt()])

        wrep = av(0, 2 * D, F32)
        osel = av(8, 2 * 512)
        zfm = av(10, 16 * 512)
        wostr = av(26, 4 * D)
        spbuf = av(34, 2 * 512, F32)
        pchk = av(38, 512)
        w("tensor", CH["Vm"])
        w("tensor", CH["PRx"])
        w("sync", CH["Dg"])
        w("sync", CH["VS"])
        dma("sync", rowb[0:1, 0:1024], I["b_outw_row"], CH["Dx"])
        dma("sync", rowb2[0:1, 0:1024], I["b_outb_row"], CH["Dx"])
        w("tensor", CH["Dx"])
        mm(PS["psR"][:, 0:512], onesf[0:1, 0:128], rowb[0:1, 0:512], True, True)
        mm(PS["psX"][:, 0:512], onesf[0:1, 0:128], rowb[0:1, 512:1024], True, True)
        mm(PS["psT"][:, 0:512], onesf[0:1, 0:128], rowb2[0:1, 0:512], True, True)
        mm(PS["psU"][:, 0:512], onesf[0:1, 0:128], rowb2[0:1, 512:1024], True, True, CH["Tm"])
        w("vector", CH["Tm"])
        ser("vector", "tensor_copy", out=wrep[:, 0:512], in_=PS["psR"][:, 0:512])
        ser("vector", "tensor_copy", out=wrep[:, 512:1024], in_=PS["psX"][:, 0:512])
        ser("vector", "tensor_copy", out=wrep[:, 1024:1536], in_=PS["psT"][:, 0:512])
        ser("vector", "tensor_copy", out=wrep[:, 1536:2048], in_=PS["psU"][:, 0:512])
        w("vector", CH["VS"])
        op("vector", "engine_nop", ch=CH["PRx"])
        op("vector", "engine_nop", ch=CH["PTx"])
        op("vector", "engine_nop", ch=CH["Vm"])
        w("gpsimd", CH["AG"])
        w("sync", CH["AG"])
        for blk in range(NBO):
            for oc in range(16):
                w("sync", CH["Tm"])
                w("sync", CH["Vm"])
                sl = oc % 2
                dma("sync", osel[:, 0:512],
                    SC["ag3o"][oc * 128:(oc + 1) * 128, blk * 512:(blk + 1) * 512], CH["Dhr"])
                dma("sync", osel[:, 512:1024],
                    SC["ag3o"][oc * 128:(oc + 1) * 128, T + blk * 512:T + (blk + 1) * 512],
                    CH["Dhr"])
                dma("sync", pchk[:, 0:512],
                    SC["pb"][oc * 128:(oc + 1) * 128, blk * 512:(blk + 1) * 512], CH["Dhr"])
                w("scalar", CH["Dhr"])
                w("scalar", CH["Vm"])
                ser("scalar", "activation", out=spbuf[:, 0:512], in_=pchk[:, 0:512], func=AF.Silu)
                if oc == 0:
                    w("vector", CH["Tm"])
                w("vector", CH["SS"])
                w("vector", CH["Dhr"])
                ser("vector", "tensor_scalar", out=spbuf[:, 512:1024], in0=osel[:, 0:512],
                    scalar1=SB["selAB"][:, 0:1], scalar2=None, op0=ALU.mult)
                ser("vector", "scalar_tensor_tensor", out=spbuf[:, 512:1024], in0=osel[:, 512:1024],
                    scalar=SB["selAB"][:, 1:2], in1=spbuf[:, 512:1024], op0=ALU.mult, op1=ALU.add)
                ser("vector", "tensor_tensor", out=zfm[:, oc * 512:(oc + 1) * 512],
                    in0=spbuf[:, 0:512], in1=spbuf[:, 512:1024], op=ALU.mult)
                w("vector", CH["VS"])
                op("vector", "engine_nop", ch=CH["Vm"])
            w("tensor", CH["Vm"])
            for i in range(4):
                tl = blk * 4 + i
                for half in range(2):
                    pb = PS["psA"] if half == 0 else PS["psB"]
                    w("tensor", CH["PAB"], CH["PAB"].n - 1)
                    for grp in range(2):
                        w("gpsimd", CH["Tm"])
                        dma("gpsimd", wostr.rearrange("p (a f) -> p a f", f=512),
                            I["bWout"].rearrange("(a p) o -> p a o", p=128)[
                                :, grp * 8:(grp + 1) * 8, half * 512:(half + 1) * 512], CH["Dwg"])
                        w("tensor", CH["Dwg"])
                        for c8 in range(8):
                            ic = grp * 8 + c8
                            mm(pb[:, 0:512], zfm[:, ic * 512 + i * 128: ic * 512 + (i + 1) * 128],
                               wostr[:, c8 * 512:(c8 + 1) * 512], ic == 0, ic == 15,
                               CH["Tm"] if c8 == 7 else None)
                    w("vector", CH["Tm"])
                    if half == 0:
                        w("vector", CH["SS"])
                    ser("vector", "tensor_copy", out=tt[:, half * 512:(half + 1) * 512],
                        in_=pb[:, 0:512])
                    w("vector", CH["VS"])
                    op("vector", "engine_nop", ch=CH["PAB"])
                w("vector", CH["VS"])
                op("vector", "engine_nop", ch=CH["Vm"])
                ln_block_sb(lambda i2: tt[:, 0:D], SB["lnstat2"], 0, 1, tt2)
                ser("vector", "tensor_tensor", out=junk[:, 0:D], in0=tt2[:, 0:D],
                    in1=wrep[:, 0:1024], op=ALU.mult)
                ser("vector", "tensor_tensor", out=tt2[:, 0:D], in0=junk[:, 0:D],
                    in1=wrep[:, 1024:2048], op=ALU.add)
                ser("vector", "tensor_tensor", out=junk[:, 0:D], in0=tt2[:, 0:D],
                    in1=xres[:, tl * D:(tl + 1) * D], op=ALU.add)
                ser("vector", "tensor_copy", out=xres[:, tl * D:(tl + 1) * D], in_=junk[:, 0:D])
                w("vector", CH["VS"])
                op("vector", "engine_nop", ch=CH["Vm"])

    if stage >= 2:
        ffn_pass(1, write_ag2=(stage >= 3))
    if stage >= 3:
        w("gpsimd", CH["Dn"])
        op("gpsimd", "collective_compute", ch=CH["AG"], kind="AllGather",
           op=ALU.bypass, replica_groups=GROUPS, ins=[SC["ag2i"].opt()], outs=[SC["ag2o"].opt()])
        attnb_pass()
    if stage >= 4:
        ffn_pass(2, write_ag2=False)

    for nm, scn in (("dbg_K", "Ka"), ("dbg_V", "Va"), ("dbg_Q", "Qa"),
                    ("dbg_h", "h"), ("dbg_h2", "h2"), ("dbg_w1q", "w1q")):
        if nm in dbg:
            for chn in ("Dd", "Dh0", "Dh1", "Dn", "Dd0", "Dd1"):
                w("gpsimd", CH[chn])
            dma("gpsimd", OUT[nm], SC[scn], CH["Dd"])
    if stage in (1, 2, 3, 4):
        w("gpsimd", CH["Vm"])
        w("gpsimd", CH["VS"])
        if PH < 7:
            op("vector", "memset", ch=CH["Vm"], ap=xres[:], constant=0.0)
            w("gpsimd", CH["Vm"])
        for tl in range(NTO):
            dma("gpsimd", OUT["y"][tl * 128:(tl + 1) * 128, :], xres[:, tl * D:(tl + 1) * D],
                CH["Dd"])
        w("gpsimd", CH["Dd"])

    for c in CH.values():
        c.sem = es.enter_context(nc.semaphore(c.name))
    with es:
        block = es.enter_context(nc.Block())
        for ename in ENGINES:
            fns = streams[ename]
            def body(E, fns=fns):
                for f in fns:
                    f(E)
            getattr(block, ename)(body)
    return nc


def make_inputs(d, core):
    b, half = core // 2, core % 2
    f32 = np.float32
    g = lambda n: np.asarray(d[n], dtype=f32)
    m = {}
    x = g("L0_attn_a_x")[b]
    m["x_full"] = np.ascontiguousarray(x)
    m["x_own"] = np.ascontiguousarray(x[half * T:(half + 1) * T])
    for wn in ("q", "k", "v", "o"):
        m[f"aW{wn}T"] = np.ascontiguousarray(g(f"L0_attn_a_W{wn}").T)
        m[f"aA{wn}T"] = np.ascontiguousarray(g(f"L0_attn_a_A{wn}").T)
        m[f"aB{wn}T"] = np.ascontiguousarray(g(f"L0_attn_a_B{wn}").T)
    m["abias"] = np.concatenate([g(f"L0_attn_a_b{wn}") for wn in ("q", "k", "v", "o")])[None, :]
    m["rz_col"] = np.broadcast_to(g("L0_attn_a_rezero").reshape(1, 1), (128, 1)).copy()
    m["w1T"] = np.ascontiguousarray(g("L0_ffn_w1").T)
    m["w2T"] = np.ascontiguousarray(g("L0_ffn_w2").T)
    cw = g("L0_ffn_conv_w")
    m["convw"] = np.ascontiguousarray(
        np.concatenate([cw[:, 0, t].reshape(32, 128).T for t in range(3)], axis=1))
    m["convb"] = np.ascontiguousarray(g("L0_ffn_conv_b").reshape(32, 128).T)
    win = g("L0_attn_b_W_in")
    m["bWq_my"] = np.ascontiguousarray(g("L0_attn_b_Wq")[:, half * 1024:(half + 1) * 1024])
    m["bWk_my"] = np.ascontiguousarray(g("L0_attn_b_Wk")[:, half * 1024:(half + 1) * 1024])
    m["bWin_v"] = np.ascontiguousarray(win[:, half * 1024:(half + 1) * 1024])
    m["bWin_p"] = np.ascontiguousarray(win[:, 2048:4096])
    m["bWout"] = np.ascontiguousarray(g("L0_attn_b_W_out"))
    m["b_lnw"] = np.ascontiguousarray(np.concatenate(
        [g(f"L0_attn_b_{p}_ln_w").reshape(8, 128).T for p in ("in", "q", "k")], axis=1))
    m["b_lnb"] = np.ascontiguousarray(np.concatenate(
        [g(f"L0_attn_b_{p}_ln_b").reshape(8, 128).T for p in ("in", "q", "k")], axis=1))
    import ml_dtypes as _mld
    m["b_bqk_row"] = np.ascontiguousarray(np.concatenate(
        [g("L0_attn_b_bq")[half * 1024:(half + 1) * 1024],
         g("L0_attn_b_bk")[half * 1024:(half + 1) * 1024]])[None, :]).astype(_mld.bfloat16)
    m["b_outw_row"] = g("L0_attn_b_out_ln_w")[None, :]
    m["b_outb_row"] = g("L0_attn_b_out_ln_b")[None, :]
    hsl = slice(half * 8, half * 8 + 8)
    m["b_smear_row"] = np.ascontiguousarray(g("L0_attn_b_smear_factor")[hsl][None, :])
    m["b_lsc_row"] = np.ascontiguousarray(g("L0_attn_b_log_scale")[hsl][None, :])
    slopes2 = 2.0 ** np.linspace(0.0, -8.0, 8)
    qpre = np.ones((8, 512), f32)
    for sl_ in range(8):
        gh = half * 8 + sl_
        if 3 <= gh <= 7:
            qpre[sl_] = np.exp(-slopes2[gh] * np.arange(512, dtype=np.float64)).astype(f32)
    m["qpre"] = qpre
    CL = [(qc, kt) for qc in range(8) for kt in range(4 * (qc + 1))]
    bt = np.ones((128, 8 * 144), f32)
    pv = np.arange(128, dtype=np.float64)
    for sl_ in range(8):
        gh = half * 8 + sl_
        if 3 <= gh <= 7:
            s_ = slopes2[gh]
            for li, (qc, kt) in enumerate(CL):
                v = np.exp(s_ * (kt * 128 + pv - qc * 512))
                v[v < 1e-38] = 0.0
                bt[:, sl_ * 144 + li] = v.astype(f32)
    m["btable"] = bt
    btile = np.zeros((3 * 49 * 128, 512), _mld.bfloat16)
    iv = np.arange(512, dtype=np.float64)
    for sl_ in range(3):
        gh = half * 8 + sl_
        for qc in range(8):
            lo = 4 * (qc + 1) - 6
            for bi in range(6):
                kt = lo + bi
                if kt < 0:
                    continue
                idx = sl_ * 49 + qc * 6 + bi
                jj = (kt * 128 + np.arange(128, dtype=np.float64))[:, None]
                ii = (qc * 512 + iv)[None, :]
                keep = (jj <= ii).astype(np.float64)
                if gh < 8:
                    tile = np.exp(slopes2[gh] * (jj - ii)) * keep
                    tile[tile < 1e-38] = 0.0
                else:
                    tile = keep
                btile[idx * 128:(idx + 1) * 128, :] = tile.astype(_mld.bfloat16)
            fidx = sl_ * 49 + 48
            btile[fidx * 128:(fidx + 1) * 128, :] = (0.0 if gh < 8 else 1.0)
    m["btile"] = btile
    tr = np.zeros((128, 5 * 512), f32)
    pj = np.arange(128)[:, None]
    fj = np.arange(512)[None, :]
    for j in range(4):
        tr[:, j * 512:(j + 1) * 512] = ((128 * j + pj) <= fj).astype(f32)
    tr[:, 4 * 512:5 * 512] = 1.0
    m["trilm"] = tr
    sel = [0.0, 0.0, 0.0, 1.0] if half == 0 else [1.0, 0.0, 0.0, 0.0]
    m["selhalo"] = np.broadcast_to(np.asarray(sel, f32)[None, :], (128, 4)).copy()
    selab = [1.0, 0.0] if half == 0 else [0.0, 1.0]
    m["selAB"] = np.broadcast_to(np.asarray(selab, f32)[None, :], (128, 2)).copy()
    m["identf"] = np.eye(128, dtype=f32)
    slopes = (0.5 ** np.arange(16)).astype(f32)
    at = np.zeros((128, 16 * 32), f32)
    pvec = np.arange(128, dtype=f32)
    for h in range(16):
        for kt in range(32):
            at[:, h * 32 + kt] = slopes[h] * (kt * 128 + pvec - 4095.0)
    m["atable"] = at
    return m


def run(inputs, stage=1, dbg=()):
    from concourse.bass_utils import run_bass_kernel_spmd
    nc = build(stage=stage, dbg=dbg)
    in_maps = [make_inputs(inputs, c) for c in range(8)]
    res = run_bass_kernel_spmd(nc, in_maps, core_ids=list(range(8)))
    return res


def kernel(**inputs):
    res = run(inputs, stage=4)
    y = np.empty((4, S, D), np.float32)
    for c in range(8):
        b, half = c // 2, c % 2
        y[b, half * T:(half + 1) * T] = res.results[c]["y"]
    return y
